# version 04
# baseline (speedup 1.0000x reference)
import contextlib
import math
from bisect import bisect_right

import numpy as np
import ml_dtypes

import concourse.bass as bass
import concourse.mybir as mybir
from concourse.bass_utils import run_bass_kernel_spmd

BF = ml_dtypes.bfloat16
F32 = mybir.dt.float32
BF16 = mybir.dt.bfloat16
FP16 = mybir.dt.float16
I16 = mybir.dt.int16
I32 = mybir.dt.int32
AF = mybir.ActivationFunctionType
ALU = mybir.AluOpType
AX = mybir.AxisListType

T = 2048
DM = 2048
NT = 16
HEADS = 16
HD = 128
CH = 64
NCH = 32
IO = 5632
NIO = 44
BH = 8
LAMBDA_INIT = 0.8 - 0.6 * math.exp(-0.3 * 12)
GLA_SCALE = HD ** -0.5
B_SCALE = 128 ** -0.5


def _sigma(nblk):
    sig = np.empty(nblk * 512, np.int64)
    p = np.arange(128)
    for b in range(nblk):
        for j in range(4):
            sig[(4 * b + j) * 128 + p] = 512 * b + 4 * p + j
    return sig


DBG_SKIP = set()
SIG = _sigma(4)
SIG2 = _sigma(11)


ENGS = ("tensor", "vector", "scalar", "gpsimd", "sync")
NSUB = 8
QUEUES = tuple(f"q_{e}_{k}" for e in ("sync", "gpsimd", "scalar") for k in range(NSUB))


class Prog:
    def __init__(self):
        self.ops = {e: [] for e in ENGS}
        self.seq = {d: 0 for d in ENGS + QUEUES}
        self.regions = {}
        self.vc = {d: {} for d in ENGS + QUEUES}
        self.snap = {}
        self.needs_inc = {e: set() for e in ENGS}
        self.last_real = {d: 0 for d in ENGS + QUEUES}
        self.qrr = {}

    def _deps_for(self, reads, writes):
        deps = set()
        for name, key in reads:
            reg = self.regions.get(name)
            if not reg:
                continue
            for k, (w, _rs) in reg.items():
                if w is not None and (key is None or k is None or key == k):
                    deps.add(w)
        for name, key in writes:
            reg = self.regions.get(name)
            if not reg:
                continue
            for k, (w, rs) in reg.items():
                if key is None or k is None or key == k:
                    if w is not None:
                        deps.add(w)
                    deps.update(rs)
        return deps

    def _update_regions(self, opid, reads, writes):
        for name, key in reads:
            reg = self.regions.setdefault(name, {})
            ent = reg.get(key)
            if ent is None:
                reg[key] = [None, [opid]]
            else:
                ent[1].append(opid)
        for name, key in writes:
            reg = self.regions.setdefault(name, {})
            if key is None:
                reg.clear()
                reg[None] = [opid, []]
            else:
                if None in reg:
                    ent = reg[None]
                reg[key] = [opid, []]

    def _merge_vc(self, dom, src_vc):
        tgt = self.vc[dom]
        for k, v in src_vc.items():
            if tgt.get(k, -1) < v:
                tgt[k] = v

    def op(self, eng, meth, *args, reads=(), writes=(), dma=False, **kwargs):
        pbr = [("pbx_" + n, None) for n, _k in reads if n.startswith("pb")]
        if pbr:
            writes = list(writes) + pbr
        deps = self._deps_for(reads, writes)
        eng_seq = self.seq[eng]
        waits = []
        myvc = self.vc[eng]
        for (d, s) in deps:
            if d == eng and eng == "tensor":
                continue
            if d not in ENGS:
                s = self.seq[d]
            if myvc.get(d, -1) >= s:
                continue
            waits.append((d, s))
        wmax = {}
        for d, s in waits:
            if wmax.get(d, -1) < s:
                wmax[d] = s
        waits = sorted(wmax.items())
        for d, s in waits:
            if d in ENGS:
                self.needs_inc[d].add(s)
            ps = self.snap.get((d, s))
            if ps:
                self._merge_vc(eng, ps)
            if myvc.get(d, -1) < s:
                myvc[d] = s
        if dma:
            self.qrr[eng] = (self.qrr.get(eng, 0) + 1) % NSUB
            q = f"q_{eng}_{self.qrr[eng]}"
            if self.seq[q] > 0 and myvc.get(q, -1) < self.seq[q]:
                waits.append((q, self.seq[q]))
                myvc[q] = self.seq[q]
            self.seq[q] += 1
            opid = (q, self.seq[q])
            self.snap[opid] = dict(myvc)
            self.snap[opid][eng] = self.last_real[eng]
            self.last_real[q] = self.seq[q]
        else:
            self.seq[eng] += 1
            opid = (eng, self.seq[eng])
            self.snap[opid] = dict(myvc)
            self.last_real[eng] = self.seq[eng]
        self._update_regions(opid, reads, writes)
        self.ops[eng].append([meth, args, kwargs, waits, opid])
        return opid

    def barrier(self):
        cur = {d: self.last_real[d] for d in ENGS + QUEUES}
        for e in ENGS:
            waits = []
            for d in ENGS + QUEUES:
                if d == e:
                    continue
                if cur[d] > 0 and self.vc[e].get(d, -1) < cur[d]:
                    waits.append((d, cur[d]))
                    if d in ENGS:
                        self.needs_inc[d].add(cur[d])
                    self.vc[e][d] = cur[d]
            self.seq[e] += 1
            opid = (e, self.seq[e])
            self.snap[opid] = dict(self.vc[e])
            self.ops[e].append([None, (), {}, sorted(waits), opid])
        for name in list(self.regions):
            self.regions[name] = {}

    def emit(self, nc, sems):
        inc_lists = {e: sorted(self.needs_inc[e]) for e in ENGS}

        def wait_val(d, s):
            if d in ENGS:
                return bisect_right(inc_lists[d], s)
            return 16 * s

        def run_engine(eng_obj, eng_name):
            for meth, args, kwargs, waits, opid in self.ops[eng_name]:
                for d, s in waits:
                    eng_obj.wait_ge(sems[d], wait_val(d, s))
                if meth is None:
                    continue
                inst = getattr(eng_obj, meth)(*args, **kwargs)
                d, s = opid
                if d in QUEUES:
                    inst.then_inc(sems[d], 16)
                elif s in self.needs_inc[eng_name]:
                    inst.then_inc(sems[d], 1)

        with nc.Block() as block:
            block.tensor(lambda e: run_engine(e, "tensor"))
            block.vector(lambda e: run_engine(e, "vector"))
            block.scalar(lambda e: run_engine(e, "scalar"))
            block.gpsimd(lambda e: run_engine(e, "gpsimd"))
            block.sync(lambda e: run_engine(e, "sync"))
        return inc_lists


def _host_pack(inputs):
    g = {}
    wq_a = inputs["L0_attn_a_Wq"][SIG, :]
    wk_a = inputs["L0_attn_a_Wk"][SIG, :]
    wv_a = inputs["L0_attn_a_Wv"][SIG, :]
    wg_a = inputs["L0_attn_a_Wg"][SIG, :]
    for nm, w in (("wq_a", wq_a), ("wk_a", wk_a), ("wv_a", wv_a), ("wg_a", wg_a)):
        g[nm] = np.ascontiguousarray(w.reshape(DM, HEADS, HD).transpose(1, 0, 2))
    g["wgk1"] = np.ascontiguousarray(inputs["L0_attn_a_Wgk1"][SIG, :])
    g["wgk2"] = np.ascontiguousarray(inputs["L0_attn_a_Wgk2"])
    g["bgk2"] = np.ascontiguousarray(inputs["L0_attn_a_bgk2"][None, :])
    g["gnw_col"] = np.ascontiguousarray(inputs["L0_attn_a_gnorm_w"][:, None])
    g["wo_a"] = np.ascontiguousarray(inputs["L0_attn_a_Wo"][:, SIG])
    wq_b = inputs["L0_attn_b_Wq"][SIG, :]
    wk_b = inputs["L0_attn_b_Wk"][SIG, :]
    for nm, w in (("wq_b", wq_b), ("wk_b", wk_b)):
        g[nm] = np.ascontiguousarray(
            w.reshape(DM, BH, 2, 128).transpose(1, 2, 0, 3).reshape(BH * 2, DM, 128))
    g["wv_b"] = np.ascontiguousarray(inputs["L0_attn_b_Wv"][SIG, :])
    g["wo_b"] = np.ascontiguousarray(inputs["L0_attn_b_Wo"][:, SIG])
    for nm in ("q1", "k1", "q2", "k2"):
        g["lam_" + nm] = np.ascontiguousarray(
            inputs["L0_attn_b_lambda_" + nm][None, :].astype(np.float32))
    g["subw"] = np.ascontiguousarray(inputs["L0_attn_b_subln_weight"][None, :])
    gup = inputs["L0_ffn_gate_up_packed"]
    gup_p = np.concatenate([gup[:IO][SIG2], gup[IO:][SIG2]], axis=0)
    g["gup16"] = np.ascontiguousarray(gup_p.view(np.int16).T)
    gsc = inputs["L0_ffn_gate_up_scales"]
    g["gsc"] = np.ascontiguousarray(
        np.concatenate([gsc[:, :IO][:, SIG2], gsc[:, IO:][:, SIG2]], axis=1))
    dp = inputs["L0_ffn_down_packed"][SIG, :]
    g["dp16"] = np.ascontiguousarray(dp.view(np.int16).T)
    g["dsc"] = np.ascontiguousarray(inputs["L0_ffn_down_scales"][:, SIG])
    g["ident"] = np.eye(128, dtype=BF)
    g["ones1"] = np.ones((1, 128), BF)
    g["ones128"] = np.ones((128, 128), BF)
    s, c = np.meshgrid(np.arange(CH), np.arange(CH), indexing="ij")
    g["amask"] = np.tile((s <= c).astype(BF), (2, 1))
    t_, tp = np.meshgrid(np.arange(CH), np.arange(CH), indexing="ij")
    g["lmat"] = np.tile(((t_ <= tp).astype(np.float32) / 16.0).astype(BF), (2, 1))
    p_, u_ = np.meshgrid(np.arange(128), np.arange(1024), indexing="ij")
    g["stair"] = (u_ >= p_ + 512).astype(BF)
    g["fconsts"] = np.array([[math.log(GLA_SCALE)]], np.float32)
    g["identf"] = np.eye(128, dtype=np.float16)
    return g


_PER_CORE = ("x",)


def _host_per_core(inputs, core):
    x = inputs["L0_attn_a_x"][core]
    return {"x": np.ascontiguousarray(x.T[SIG, :])}


def build_nc(parts=("gla", "ffn1", "attnb", "ffn2"), upto=None):
    nc = bass.Bass("TRN2", target_bir_lowering=False, debug=False)
    pg = Prog()

    def din(name, shape, dt):
        return nc.dram_tensor(name, list(shape), dt, kind="ExternalInput").ap()

    D = {}
    D["x"] = din("x", (DM, T), F32)
    for nm in ("wq_a", "wk_a", "wv_a", "wg_a"):
        D[nm] = din(nm, (HEADS, DM, HD), F32)
    D["wgk1"] = din("wgk1", (DM, 16), F32)
    D["wgk2"] = din("wgk2", (16, DM), F32)
    D["bgk2"] = din("bgk2", (1, DM), F32)
    D["gnw_col"] = din("gnw_col", (128, 1), F32)
    D["wo_a"] = din("wo_a", (DM, DM), F32)
    for nm in ("wq_b", "wk_b"):
        D[nm] = din(nm, (16, DM, 128), F32)
    D["wv_b"] = din("wv_b", (DM, DM), F32)
    D["wo_b"] = din("wo_b", (DM, DM), F32)
    for nm in ("q1", "k1", "q2", "k2"):
        D["lam_" + nm] = din("lam_" + nm, (1, 128), F32)
    D["subw"] = din("subw", (1, 256), F32)
    D["gup16"] = din("gup16", (512, 2 * IO), I16)
    D["gsc"] = din("gsc", (16, 2 * IO), FP16)
    D["dp16"] = din("dp16", (1408, DM), I16)
    D["dsc"] = din("dsc", (NIO, DM), FP16)
    for nm in ("ident", "ones1", "ones128"):
        D[nm] = din(nm, (128, 128) if nm != "ones1" else (1, 128), BF16)
    D["amask"] = din("amask", (128, CH), BF16)
    D["lmat"] = din("lmat", (128, CH), BF16)
    D["stair"] = din("stair", (128, 1024), BF16)
    D["fconsts"] = din("fconsts", (1, 1), F32)
    D["identf"] = din("identf", (128, 128), FP16)
    D["y"] = nc.dram_tensor("y", [DM, T], F32, kind="ExternalOutput").ap()

    def scratch(name, shape, dt):
        return nc.dram_tensor(name, list(shape), dt, kind="Internal").ap()

    D["lam_sc"] = scratch("lam_sc", (1, 4), F32)
    D["residA"] = scratch("residA", (DM, T), F32)
    D["residB"] = scratch("residB", (DM, T), F32)
    D["gate_sp"] = scratch("gate_sp", (HEADS, HD, T), BF16)
    D["gk_sp"] = scratch("gk_sp", (HEADS, 128, 16 * HD), BF16)
    D["o_sp"] = scratch("o_sp", (HEADS, HD, T), BF16)
    D["a_sp"] = scratch("a_sp", (BH, 2, 128, T), BF16)
    D["h_sp"] = scratch("h_sp", (NIO, 128, T), BF16)
    D["wgu_sc"] = scratch("wgu_sc", (88, 128, 16 * 128), BF16)
    D["wd_sc"] = scratch("wd_sc", (16, 128, NIO * 128), BF16)
    D["wo_a_bf"] = scratch("wo_a_bf", (16, 128, 16 * 128), BF16)
    D["wo_b_bf"] = scratch("wo_b_bf", (16, 128, 16 * 128), BF16)

    al = nc.alloc_sbuf_tensor
    xn = al("xn", [128, NT, T], BF16).ap()
    rhold = al("rhold", [128, NT, 512], BF16).ap()
    mu_b = al("mu_b", [128, 512], F32).ap()
    xtmp = al("xtmp", [128, 512], F32).ap()
    rstd_b = al("rstd_b", [128, 512], F32).ap()
    var_b = al("var_b", [128, 512], F32).ap()
    c_ident = al("c_ident", [128, 128], BF16).ap()
    c_ones1 = al("c_ones1", [1, 128], BF16).ap()
    c_ones128 = al("c_ones128", [128, 128], BF16).ap()
    c_amask = al("c_amask", [128, CH], BF16).ap()
    c_lmat = al("c_lmat", [128, CH], BF16).ap()
    c_stair = al("c_stair", [128, 1024], BF16).ap()
    c_gnw = al("c_gnw", [128, 1], F32).ap()
    c_wgk1 = al("c_wgk1", [128, 16, 16], BF16).ap()
    c_identf = al("c_identf", [128, 128], FP16).ap()
    lam_cols = al("lam_cols", [128, 4], F32).ap()
    sw_cols = al("sw_cols", [128, 2], F32).ap()
    lam_rows = al("lam_rows", [1, 512], F32).ap()
    lam_tmp = al("lam_tmp", [1, 520], F32).ap()
    c_fconst = al("c_fconst", [128, 1], F32).ap()
    nc.const_aps.register(F32, math.log(GLA_SCALE), c_fconst[:, 0:1])

    ARENA_B = 103 * 1024
    arena_t = al("arena", [128, ARENA_B], mybir.dt.uint8)
    arena = arena_t.ap()

    def av(off, shape, dt, base=0):
        nb = int(np.prod(shape[1:])) * mybir.dt.size(dt)
        v = arena[base: base + shape[0], off: off + nb].bitcast(dt)
        if len(shape) > 2:
            ix = "p (" + " ".join(f"d{i}" for i in range(1, len(shape))) + ") -> p " + \
                 " ".join(f"d{i}" for i in range(1, len(shape)))
            v = v.rearrange(ix, **{f"d{i}": shape[i] for i in range(1, len(shape) - 1)})
        return v

    pbank = [nc.alloc_psum_tensor(f"pb{i}", [128, 512], F32).ap() for i in range(8)]
    c_wgk2 = av(80 * 1024, [16, DM], BF16)
    c_bgk2 = av(84 * 1024, [1, DM], BF16)

    def mm(out, lhsT, rhs, rd, wr, start=True, stop=True):
        pg.op("tensor", "matmul", out, lhsT, rhs, start=start, stop=stop,
              reads=rd, writes=wr)

    def trans(out, in_, rd, wr, ident=None, ident_key=("c_ident", None)):
        idap = c_ident if ident is None else ident
        pg.op("tensor", "transpose", out, in_, idap, reads=list(rd) + [ident_key], writes=wr)

    def dma(eng, dst, src, rd, wr, **kw):
        return pg.op(eng, "dma_start", dst, src, dma=True, reads=rd, writes=wr, **kw)

    def act(out, in_, fn, rd, wr, eng="scalar", **kw):
        pg.op(eng, "activation", out, in_, fn, reads=rd, writes=wr, **kw)

    def vop(meth, *args, rd=(), wr=(), eng="vector", **kw):
        pg.op(eng, meth, *args, reads=rd, writes=wr, **kw)

    def quake_rsqrt(dst, src, n_newton=3, tmp=None, rd=(), wr=(), tmp_key=("xtmp", None)):
        tk = [tmp_key]
        vop("tensor_scalar", dst.bitcast(I32), src.bitcast(I32), 1, None,
            ALU.logical_shift_right, rd=rd, wr=wr)
        vop("tensor_scalar", dst.bitcast(I32), dst.bitcast(I32), -1, 0x5F3759DF,
            ALU.mult, ALU.add, rd=wr, wr=wr)
        for _ in range(n_newton):
            vop("tensor_tensor", tmp, dst, dst, ALU.mult, rd=wr, wr=tk)
            vop("tensor_tensor", tmp, tmp, src, ALU.mult, rd=list(rd) + tk, wr=tk)
            vop("tensor_scalar", tmp, tmp, -0.5, 1.5, ALU.mult, ALU.add, rd=tk, wr=tk)
            vop("tensor_tensor", dst, dst, tmp, ALU.mult, rd=list(wr) + tk, wr=wr)

    for nm, dst in (("ident", c_ident), ("ones1", c_ones1), ("ones128", c_ones128),
                    ("amask", c_amask), ("lmat", c_lmat), ("stair", c_stair),
                    ("gnw_col", c_gnw)):
        dma("scalar", dst, D[nm], [(nm, None)], [("c_" + nm, None)])
    dma("scalar", c_fconst, D["fconsts"][0:1, 0:1].partition_broadcast(128),
        [("fconsts", None)], [("c_fconst", None)])
    dma("scalar", c_identf, D["identf"], [("identf", None)], [("c_identf", None)])
    dma("gpsimd", c_bgk2, D["bgk2"], [("bgk2", None)], [("c_bgk2", None)])
    dma("gpsimd", c_wgk2, D["wgk2"], [("wgk2", None)], [("c_wgk2", None)])
    dma("gpsimd", c_wgk1, D["wgk1"].rearrange("(a p) r -> p a r", p=128),
        [("wgk1", None)], [("c_wgk1", None)])
    for i, nm in enumerate(("q1", "k1", "q2", "k2")):
        dma("scalar", lam_rows[:, 128 * i:128 * (i + 1)], D["lam_" + nm],
            [("lam_" + nm, None)], [("lam_rows", i)])
    vop("tensor_tensor", lam_tmp[:, 0:128], lam_rows[:, 0:128], lam_rows[:, 128:256], ALU.mult,
        rd=[("lam_rows", 0), ("lam_rows", 1)], wr=[("lam_tmp", 0)])
    vop("tensor_tensor", lam_tmp[:, 128:256], lam_rows[:, 256:384], lam_rows[:, 384:512], ALU.mult,
        rd=[("lam_rows", 2), ("lam_rows", 3)], wr=[("lam_tmp", 1)])
    vop("reduce_sum", lam_tmp[:, 512:513], lam_tmp[:, 0:128], AX.X, rd=[("lam_tmp", 0)], wr=[("lam_tmp", 2)])
    vop("reduce_sum", lam_tmp[:, 513:514], lam_tmp[:, 128:256], AX.X, rd=[("lam_tmp", 1)], wr=[("lam_tmp", 3)])
    act(lam_tmp[:, 516:518], lam_tmp[:, 512:514], AF.Exp, [("lam_tmp", 2), ("lam_tmp", 3)], [("lam_tmp", 4)])
    vop("tensor_tensor", lam_tmp[:, 518:519], lam_tmp[:, 517:518], lam_tmp[:, 516:517], ALU.subtract,
        rd=[("lam_tmp", 4)], wr=[("lam_tmp", 5)])
    vop("tensor_scalar", lam_tmp[:, 518:519], lam_tmp[:, 518:519], LAMBDA_INIT, None, ALU.subtract,
        rd=[("lam_tmp", 5)], wr=[("lam_tmp", 5)])
    dma("scalar", D["lam_sc"][0:1, 0:1], lam_tmp[:, 518:519], [("lam_tmp", 5)], [("lam_sc", None)])
    dma("scalar", lam_cols[:, 0:1], D["lam_sc"][0:1, 0:1].partition_broadcast(128),
        [("lam_sc", None)], [("lam_cols", None)])
    dma("scalar", sw_cols, D["subw"].rearrange("o (r p) -> p (o r)", p=128),
        [("subw", None)], [("sw_cols", 0)], allow_slow_non_contiguous=True)
    vop("tensor_scalar", sw_cols, sw_cols, 1.0 - LAMBDA_INIT, None, ALU.mult,
        rd=[("sw_cols", 0)], wr=[("sw_cols", 0)])

    wobuf = [av(0, [128, T], BF16), av(4096, [128, T], BF16)]
    for wsrc, wdst, scaled in (("wo_a", "wo_a_bf", False), ("wo_b", "wo_b_bf", True)):
        for r in range(16):
            b = r % 2
            dma("gpsimd", wobuf[b], D[wsrc][128 * r:128 * (r + 1), :],
                [(wsrc, None)], [("wobuf", b)])
            if scaled:
                vop("tensor_scalar", wobuf[b], wobuf[b], sw_cols[:, r % 2:r % 2 + 1], None, ALU.mult,
                    rd=[("wobuf", b), ("sw_cols", 0)], wr=[("wobuf", b)])
            for o2 in range(16):
                dma("sync", D[wdst][o2, :, 128 * r:128 * (r + 1)], wobuf[b][:, 128 * o2:128 * (o2 + 1)],
                    [("wobuf", b)], [(wdst, (o2, r))])

    xhold = av(8192, [128, NT, 512], F32)
    xbf_t = av(8192 + 33 * 1024, [128, 512], BF16)
    xsq_t = av(8192 + 34 * 1024, [128, 512], BF16)

    def ln_stats_apply(tb, tiles, eps, mean_ps, sq_ps, mean_k, sq_k):
        vop("tensor_scalar", mu_b, mean_ps, 1.0 / DM, None, ALU.mult,
            rd=[(mean_k, None)], wr=[("mu_b", None)])
        vop("tensor_scalar", var_b, sq_ps, 1.0 / DM, None, ALU.mult,
            rd=[(sq_k, None)], wr=[("var_b", None)])
        vop("tensor_tensor", xtmp, mu_b, mu_b, ALU.mult, rd=[("mu_b", None)], wr=[("xtmp", None)])
        vop("tensor_tensor", var_b, var_b, xtmp, ALU.subtract,
            rd=[("var_b", None), ("xtmp", None)], wr=[("var_b", None)])
        vop("tensor_scalar", var_b, var_b, eps, None, ALU.add, rd=[("var_b", None)], wr=[("var_b", None)])
        quake_rsqrt(rstd_b, var_b, tmp=xtmp, rd=[("var_b", None)], wr=[("rstd_b", None)])
        for it, (nm, key, ap) in enumerate(tiles):
            vop("tensor_tensor", xtmp, ap, mu_b, ALU.subtract,
                rd=[(nm, key), ("mu_b", None)], wr=[("xtmp", None)])
            vop("tensor_tensor", xn[:, it, 512 * tb:512 * (tb + 1)], xtmp, rstd_b, ALU.mult,
                rd=[("xtmp", None), ("rstd_b", None)], wr=[("xn", (it, tb))])

    for tb in range(4):
        mean_ps, sq_ps = pbank[0], pbank[1]
        for it in range(16):
            dma("sync", xhold[:, it, :], D["x"][128 * it:128 * (it + 1), 512 * tb:512 * (tb + 1)],
                [("x", None)], [("xhold", it)])
            pg.op("scalar", "copy", xbf_t, xhold[:, it, :], reads=[("xhold", it)], writes=[("xbf_t", None)])
            pg.op("scalar", "square", xsq_t, xbf_t, reads=[("xbf_t", None)], writes=[("xsq_t", None)])
            mm(mean_ps, c_ones128, xbf_t, [("c_ones128", None), ("xbf_t", None)], [("pb0", None)],
               start=(it == 0), stop=(it == 15))
            mm(sq_ps, c_ones128, xsq_t, [("c_ones128", None), ("xsq_t", None)], [("pb1", None)],
               start=(it == 0), stop=(it == 15))
        ln_stats_apply(tb, [("xhold", it, xhold[:, it, :]) for it in range(16)],
                       1e-5, mean_ps, sq_ps, "pb0", "pb1")

    pg.barrier()

    def debug_finish():
        for it in range(16):
            pg.op("scalar", "copy", xhold[:, it, :], xn[:, it, 0:512],
                  reads=[("xn", (it, 0))], writes=[("xhold", it)])
            for tb in range(4):
                dma("sync", D["y"][128 * it:128 * (it + 1), 512 * tb:512 * (tb + 1)],
                    xhold[:, it, :], [("xhold", it)], [("ydbg", (it, tb))])
        pg.barrier()

    if upto == "p0":
        debug_finish()
        es = contextlib.ExitStack()
        sems = {}
        for d in ENGS + QUEUES:
            sems[d] = es.enter_context(nc.semaphore("sem_" + d))
        pg.emit(nc, sems)
        return nc

    LNSC = math.log(GLA_SCALE)
    do_gla = "gla" in parts
    if do_gla:
        wgb = [av(48 * 1024, [128, 16, 128], BF16), av(52 * 1024, [128, 16, 128], BF16)]
        gstage = [av(56 * 1024, [128, T], BF16), av(60 * 1024, [128, T], BF16)]
        for h in range(HEADS):
            b = h % 2
            dma("gpsimd", wgb[b], D["wg_a"][h].rearrange("(a p) o -> p a o", p=128),
                [("wg_a", None)], [("wgb", b)])
            for tb in range(4):
                ps = pbank[4 * b + tb]
                for ik in range(16):
                    mm(ps, wgb[b][:, ik, :], xn[:, ik, 512 * tb:512 * (tb + 1)],
                       [("wgb", b), ("xn", (ik, tb))], [(f"pb{4*b+tb}", None)],
                       start=(ik == 0), stop=(ik == 15))
                act(gstage[b][:, 512 * tb:512 * (tb + 1)], ps, AF.Silu,
                    [(f"pb{4*b+tb}", None)], [("gstage", b)])
            dma("sync", D["gate_sp"][h], gstage[b], [("gstage", b)], [("gate_sp", h)])

        if upto == "gate":
            debug_finish()
            es = contextlib.ExitStack()
            sems = {}
            for d in ENGS + QUEUES:
                sems[d] = es.enter_context(nc.semaphore("sem_" + d))
            pg.emit(nc, sems)
            return nc
        zbuf = av(64 * 1024, [16, T], BF16)
        g_t1 = av(68 * 1024, [128, 512], F32)
        g_t2 = av(70 * 1024, [128, 512], F32)
        gkst = [av(72 * 1024, [128, 16, 128], BF16), av(76 * 1024, [128, 16, 128], BF16)]
        for tb in range(4):
            ps = pbank[0][:16, :]
            for ik in range(16):
                mm(ps, c_wgk1[:, ik, :], xn[:, ik, 512 * tb:512 * (tb + 1)],
                   [("c_wgk1", None), ("xn", (ik, tb))], [("pb0", None)],
                   start=(ik == 0), stop=(ik == 15))
            vop("tensor_copy", zbuf[:, 512 * tb:512 * (tb + 1)], ps,
                rd=[("pb0", None)], wr=[("zbuf", tb)])
        if upto == "gkz":
            debug_finish()
            es = contextlib.ExitStack()
            sems = {}
            for d in ENGS + QUEUES:
                sems[d] = es.enter_context(nc.semaphore("sem_" + d))
            pg.emit(nc, sems)
            return nc
        for h in range(HEADS if upto != "gk1" else 2):
            b = h % 2
            for g4 in range(4):
                ps = pbank[1 + (g4 % 2)]
                for q in range(4):
                    tt = 4 * g4 + q
                    pv = ps[:, 128 * q:128 * (q + 1)]
                    if "gkbias" not in DBG_SKIP:
                        mm(pv, c_ones1, c_bgk2[:, 128 * h:128 * (h + 1)],
                           [("c_ones1", None), ("c_bgk2", None)], [(f"pb{1+(g4%2)}", q)],
                           start=True, stop=False)
                    mm(pv, zbuf[:, 128 * tt:128 * (tt + 1)], c_wgk2[:, 128 * h:128 * (h + 1)],
                       [("zbuf", tt // 4), ("c_wgk2", None)], [(f"pb{1+(g4%2)}", q)],
                       start=("gkbias" in DBG_SKIP), stop=True)
                pk = [(f"pb{1+(g4%2)}", q) for q in range(4)]
                act(g_t1, ps, AF.Abs, pk, [("g_t1", None)])
                vop("tensor_scalar", g_t2, ps, 0.0, None, ALU.min,
                    rd=pk + [("g_t1", None)], wr=[("g_t2", None)])
                if "lnexp" not in DBG_SKIP:
                    act(g_t1, g_t1, AF.Exp, [("g_t1", None)], [("g_t1", None)], scale=-1.0)
                    act(g_t1, g_t1, AF.Ln, [("g_t1", None)], [("g_t1", None)], bias=1.0)
                vop("scalar_tensor_tensor", gkst[b][:, 4 * g4:4 * (g4 + 1), :], g_t1, -1.0, g_t2,
                    ALU.mult, ALU.add, rd=[("g_t1", None), ("g_t2", None)], wr=[("gkst", b)])
            if "gkspill" not in DBG_SKIP:
                dma("sync", D["gk_sp"][h], gkst[b].rearrange("p a c -> p (a c)"),
                    [("gkst", b)], [("gk_sp", h)])

        if upto == "gk1":
            debug_finish()
            es = contextlib.ExitStack()
            sems = {}
            for d in ENGS + QUEUES:
                sems[d] = es.enter_context(nc.semaphore("sem_" + d))
            pg.emit(nc, sems)
            return nc
        pg.barrier()
        if upto == "gk":
            debug_finish()
            es = contextlib.ExitStack()
            sems = {}
            for d in ENGS + QUEUES:
                sems[d] = es.enter_context(nc.semaphore("sem_" + d))
            pg.emit(nc, sems)
            return nc

        qg = [av(0, [128, T], BF16), av(4096, [128, T], BF16)]
        qh = [av(8192, [128, T], BF16), av(12288, [128, T], BF16)]
        kh = [av(16384, [128, T], BF16), av(20480, [128, T], BF16)]
        qraw = av(24576, [128, T], BF16)
        kraw = av(28672, [128, T], BF16)
        gktm_lo = av(32768, [64, 16, 128], BF16)
        gktm_hi = av(36864, [64, 16, 128], BF16)
        gatebuf = av(40960, [128, T], BF16)
        f_t1 = av(45056, [128, 512], F32)
        f_t2 = av(47104, [128, 512], F32)
        vT = av(49152, [128, T], BF16)
        vtm = av(53248, [64, 32, 128], BF16)
        kwtm = av(61440, [64, 32, 128], BF16)
        gst = av(69632, [128, T], FP16)
        otm = [av(73728, [64, 32, 128], FP16), av(81920, [64, 32, 128], FP16)]
        oT = av(90112, [128, T], BF16)
        wbuf = [av(94208, [128, 16, 128], BF16), av(98304, [128, 16, 128], BF16)]
        S_f = av(102400, [128, 128], F32)
        S_bf = av(102912, [128, 128], BF16)
        eglall = [av(103168, [128, 32], F32), av(103296, [128, 32], F32)]
        ssqh = av(103424, [64, 32], F32)
        rstdh = av(103552, [64, 32], F32)
        qtk = av(103680, [64, 32], F32)
        atb = [av(103808, [64, 64], BF16), av(103936, [64, 64], BF16)]
        assert 103936 + 128 <= ARENA_B

        def emit_chunk(g, c):
            if "chunks" in DBG_SKIP:
                return
            gp = g % 2
            cp = c % 2
            cs = slice(64 * c, 64 * (c + 1))
            psA = pbank[3 + cp][0:64, 0:64]
            mm(psA, kh[gp][:, cs], qh[gp][:, cs],
               [(f"kh{gp}", None), (f"qh{gp}", None)], [(f"pb{3+cp}", None)])
            vop("tensor_tensor", atb[cp], psA, c_amask[0:64, :], ALU.mult,
                rd=[(f"pb{3+cp}", None), ("c_amask", None)], wr=[("atb", cp)])
            pso = pbank[5][0:64, 0:128]
            vch = vtm[:, c, :]
            mm(pso, atb[cp], vch, [("atb", cp), ("vtm", None)], [("pb5", None)],
               start=True, stop=(c == 0))
            if c > 0:
                mm(pso, qg[gp][:, cs], S_bf, [(f"qg{gp}", None), ("S_bf", None)], [("pb5", None)],
                   start=False, stop=True)
            psS = pbank[5][:, 128:256]
            kch = kwtm[:, c, :]
            mm(psS, kch, vch, [("kwtm", None), ("vtm", None)], [("pb5", None)])
            if c == 0:
                vop("tensor_copy", S_f, psS, rd=[("pb5", None)], wr=[("S_f", None)])
            else:
                vop("scalar_tensor_tensor", S_f, S_f, eglall[gp][:, c:c + 1], psS, ALU.mult, ALU.add,
                    rd=[("S_f", None), (f"egl{gp}", None), ("pb5", None)], wr=[("S_f", None)])
            vop("tensor_copy", S_bf, S_f, rd=[("S_f", None)], wr=[("S_bf", None)])
            vop("tensor_copy", otm[gp][:, c, :], pso, rd=[("pb5", None)], wr=[(f"otm{gp}", None)])

        def proj_items(h):
            for wi, (wname, dstname, dst) in enumerate(
                    (("wq_a", "qraw", qraw), ("wk_a", "kraw", kraw), ("wv_a", "vT", vT))):
                wb = wbuf[wi % 2]
                wkey = ("wbuf", wi % 2)

                def _ld(wb=wb, wname=wname, wkey=wkey):
                    dma("gpsimd", wb, D[wname][h].rearrange("(a p) o -> p a o", p=128),
                        [(wname, None)], [wkey])
                yield (0, _ld)
                for tb in range(4):
                    bk = tb % 2
                    ps = pbank[bk]
                    for ik in range(16):
                        def _mm(ps=ps, wb=wb, wkey=wkey, ik=ik, tb=tb, bk=bk):
                            mm(ps, wb[:, ik, :], xn[:, ik, 512 * tb:512 * (tb + 1)],
                               [wkey, ("xn", (ik, tb))], [(f"pb{bk}", None)],
                               start=(ik == 0), stop=(ik == 15))
                        yield (1, _mm)

                    def _ev(dst=dst, ps=ps, tb=tb, bk=bk, dstname=dstname):
                        pg.op("scalar", "copy", dst[:, 512 * tb:512 * (tb + 1)], ps,
                              reads=[(f"pb{bk}", None)], writes=[(dstname, None)])
                    yield (0, _ev)

        def run_items(items, budget):
            n = 0
            while n < budget:
                it = next(items, None)
                if it is None:
                    return False
                it[1]()
                n += it[0]
            return True

        def factor_items(h):
            if "factorsx" in DBG_SKIP:
                return
            pr = h % 2
            for blk in range(4):
                sl = slice(512 * blk, 512 * (blk + 1))

                def _f(pr=pr, sl=sl, blk=blk):
                    act(f_t1, gst[:, sl], AF.Exp, [("gst", None), ("c_fconst", None)],
                        [("f_t1", None)], bias=LNSC)
                    vop("tensor_tensor", qg[pr][:, sl], qraw[:, sl], f_t1, ALU.mult,
                        rd=[("qraw", None), ("f_t1", None)], wr=[(f"qg{pr}", None)])
                    vop("tensor_copy", f_t2.rearrange("p (a z) -> p a z", a=8),
                        gst[:, 512 * blk:512 * (blk + 1):64].unsqueeze(2).broadcast_to([128, 8, 64]),
                        rd=[("gst", None)], wr=[("f_t2", None)])
                    vop("tensor_tensor", f_t2, gst[:, sl], f_t2, ALU.subtract,
                        rd=[("gst", None), ("f_t2", None)], wr=[("f_t2", None)])
                    act(f_t1, f_t2, AF.Exp, [("f_t2", None), ("c_fconst", None)],
                        [("f_t1", None)], bias=LNSC)
                    vop("tensor_tensor", qh[pr][:, sl], qraw[:, sl], f_t1, ALU.mult,
                        rd=[("qraw", None), ("f_t1", None)], wr=[(f"qh{pr}", None)])
                    act(f_t1, f_t2, AF.Exp, [("f_t2", None)], [("f_t1", None)], scale=-1.0)
                    vop("tensor_tensor", kh[pr][:, sl], kraw[:, sl], f_t1, ALU.mult,
                        rd=[("kraw", None), ("f_t1", None)], wr=[(f"kh{pr}", None)])
                    vop("tensor_copy", f_t2.rearrange("p (a z) -> p a z", a=8),
                        gst[:, 512 * blk + 63:512 * (blk + 1):64].unsqueeze(2).broadcast_to([128, 8, 64]),
                        rd=[("gst", None)], wr=[("f_t2", None)])
                    vop("tensor_tensor", f_t2, f_t2, gst[:, sl], ALU.subtract,
                        rd=[("gst", None), ("f_t2", None)], wr=[("f_t2", None)])
                    act(f_t1, f_t2, AF.Exp, [("f_t2", None)], [("f_t1", None)])
                    vop("tensor_tensor", qraw[:, sl], kraw[:, sl], f_t1, ALU.mult,
                        rd=[("kraw", None), ("f_t1", None)], wr=[("qraw", None)])
                yield (0, _f)

            def _egl(pr=pr):
                act(eglall[pr], gst[:, 63::64], AF.Exp, [("gst", None)], [(f"egl{pr}", None)])
            yield (0, _egl)

        def cumsum_items(h):
            for g8 in range(4):
                for c8 in range(8):
                    c = 8 * g8 + c8

                    def _cmm(c=c, c8=c8):
                        gsrc = gktm_lo if c % 2 == 0 else gktm_hi
                        gname = "gktm_lo" if c % 2 == 0 else "gktm_hi"
                        mm(pbank[2][:, 64 * c8:64 * (c8 + 1)],
                           gsrc[:, c // 2, :], c_lmat[0:64, :],
                           [(gname, None), ("c_lmat", None)], [("pb2", c8)])
                    yield (1, _cmm)

                def _cev(g8=g8):
                    vop("tensor_copy", gst[:, 512 * g8:512 * (g8 + 1)], pbank[2],
                        rd=[("pb2", i) for i in range(8)], wr=[("gst", None)])
                yield (0, _cev)

        def gnorm_items(g):
            po = g % 2
            otg = otm[po]
            otflat = otg.rearrange("p a c -> p (a c)")
            if "gnormx" in DBG_SKIP:
                def _sp0():
                    dma("sync", D["o_sp"][g], oT, [("oT", None)], [("o_sp", g)])
                yield (0, _sp0)
                return
            for blk in range(8):
                def _sq(blk=blk, otflat=otflat):
                    sl = slice(512 * blk, 512 * (blk + 1))
                    vop("tensor_tensor", f_t1[0:64, :], otflat[:, sl], otflat[:, sl], ALU.mult,
                        rd=[(f"otm{po}", None)], wr=[("f_t1", None)])
                    vop("reduce_sum", ssqh[:, 4 * blk:4 * (blk + 1)].unsqueeze(2),
                        f_t1[0:64, :].rearrange("p (a z) -> p a z", a=4), AX.X,
                        rd=[("f_t1", None)], wr=[("ssqh", None)])
                yield (0, _sq)

            def _rstd():
                vop("tensor_scalar", ssqh, ssqh, 1.0 / HD, 1e-5, ALU.mult, ALU.add,
                    rd=[("ssqh", None)], wr=[("ssqh", None)])
                quake_rsqrt(rstdh, ssqh, tmp=qtk, rd=[("ssqh", None)], wr=[("rstdh", None)],
                            tmp_key=("qtk", None))
            yield (0, _rstd)
            for c in range(32):
                def _ap(c=c, otg=otg):
                    vop("tensor_scalar", otg[:, c, :], otg[:, c, :],
                        rstdh[:, c:c + 1], None, ALU.mult,
                        rd=[(f"otm{po}", None), ("rstdh", None)], wr=[(f"otm{po}", None)])
                yield (0, _ap)
            for c in range(32):
                def _tr(c=c, otg=otg):
                    bk = 6 + c % 2
                    pv = pbank[bk][:, 0:32].bitcast(FP16)
                    trans(pv, otg[:, c, :], [(f"otm{po}", None)], [(f"pb{bk}", None)],
                          ident=c_identf[0:64, 0:64], ident_key=("c_identf", None))
                    vop("scalar_tensor_tensor", oT[:, 64 * c:64 * (c + 1)], pv, c_gnw[:, 0:1],
                        gatebuf[:, 64 * c:64 * (c + 1)], ALU.mult, ALU.mult,
                        rd=[(f"pb{bk}", None), ("c_gnw", None), ("gatebuf", None)], wr=[("oT", None)])
                yield (1, _tr)

            def _sp():
                dma("sync", D["o_sp"][g], oT, [("oT", None)], [("o_sp", g)])
                dma("sync", gatebuf, D["gate_sp"][g + 1], [("gate_sp", g + 1)], [("gatebuf", None)])
            if g + 1 < HEADS:
                yield (0, _sp)
            else:
                yield (0, lambda: dma("sync", D["o_sp"][g], oT, [("oT", None)], [("o_sp", g)]))

        def emit_transposes(h):
            if "transx" in DBG_SKIP:
                return
            for c in range(NCH):
                bk = 6 + c % 2
                pv = pbank[bk][0:64, 0:64].bitcast(BF16)
                trans(pv, qraw[:, 64 * c:64 * (c + 1)], [("qraw", None)], [(f"pb{bk}", None)])
                vop("tensor_copy", kwtm[:, c, :], pv, rd=[(f"pb{bk}", None)], wr=[("kwtm", None)])
            for c in range(NCH):
                bk = 6 + c % 2
                pv = pbank[bk][0:64, 0:64].bitcast(BF16)
                trans(pv, vT[:, 64 * c:64 * (c + 1)], [("vT", None)], [(f"pb{bk}", None)])
                vop("tensor_copy", vtm[:, c, :], pv, rd=[(f"pb{bk}", None)], wr=[("vtm", None)])

        def head_items(h):
            if h >= 2:
                yield from gnorm_items(h - 2)
            if h < HEADS:
                yield from cumsum_items(h)
                yield from proj_items(h)
                yield from factor_items(h)

        dma("sync", gatebuf, D["gate_sp"][0], [("gate_sp", 0)], [("gatebuf", None)])
        for h in range(HEADS + 2):
            if h < HEADS:
                dma("sync", gktm_lo.rearrange("p a c -> p (a c)"), D["gk_sp"][h][0:64, :],
                    [("gk_sp", h)], [("gktm_lo", None)])
                dma("sync", gktm_hi.rearrange("p a c -> p (a c)"), D["gk_sp"][h][64:128, :],
                    [("gk_sp", h)], [("gktm_hi", None)])
            items = head_items(h)
            if h > 0 and h <= HEADS:
                g = h - 1
                run_items(items, 24)
                for c in range(NCH):
                    emit_chunk(g, c)
                    run_items(items, 8)
            while run_items(items, 64):
                pass
            if h < HEADS:
                emit_transposes(h)

        if upto == "dump1":
            pg.barrier()
            for i, srcap in enumerate((D["gate_sp"][0], D["gk_sp"][0], D["o_sp"][0],
                                       D["o_sp"][15], D["gate_sp"][15], D["gk_sp"][15])):
                dma("gpsimd", D["y"][128 * i:128 * (i + 1), :], srcap, [("dsrc", i)], [("ydbg", i)])
            pg.barrier()
            es = contextlib.ExitStack()
            sems = {}
            for d in ENGS + QUEUES:
                sems[d] = es.enter_context(nc.semaphore("sem_" + d))
            pg.emit(nc, sems)
            return nc

        pg.barrier()
        if upto == "heads":
            debug_finish()
            es = contextlib.ExitStack()
            sems = {}
            for d in ENGS + QUEUES:
                sems[d] = es.enter_context(nc.semaphore("sem_" + d))
            pg.emit(nc, sems)
            return nc


    def wo_phase(src_tiles, wo_sc, resid_src, resid_dst, do_ln, tag):
        ob = [av(0, [128, 16, 512], BF16, ), av(16 * 1024, [128, 16, 512], BF16)]
        wot = [av(32 * 1024, [128, 16, 128], BF16), av(36 * 1024, [128, 16, 128], BF16)]
        ytmp = [av(40 * 1024, [128, 512], F32), av(49 * 1024, [128, 512], F32)]
        sqst = av(42 * 1024, [128, 512], BF16)
        xres = [av(43 * 1024 + 2048 * i, [128, 512], F32) for i in range(3)]
        for tb in range(4):
            obb = ob[tb % 2]
            for r in range(16):
                dma("sync", obb[:, r, :], src_tiles[r][:, 512 * tb:512 * (tb + 1)],
                    [(f"{tag}src", r)], [(f"{tag}ob", (tb % 2, r))])
            for q in range(4):
                for oi in range(4):
                    o2 = 4 * q + oi
                    wb = wot[o2 % 2]
                    dma("sync", wb, D[wo_sc][o2], [(wo_sc, None)], [(f"{tag}wot", o2 % 2)])
                    for hd in range(16):
                        mm(pbank[oi], wb[:, hd, :], obb[:, hd, :],
                           [(f"{tag}wot", o2 % 2), (f"{tag}ob", (tb % 2, hd))],
                           [(f"pb{oi}", None)], start=(hd == 0), stop=(hd == 15))
                    dma("sync", xres[o2 % 3], resid_src(o2, tb), [("residsrc", None)],
                        [(f"{tag}xres", o2 % 3)])
                    yt = ytmp[o2 % 2]
                    vop("tensor_tensor", yt, pbank[oi], xres[o2 % 3], ALU.add,
                        rd=[(f"pb{oi}", None), (f"{tag}xres", o2 % 3)], wr=[(f"{tag}yt", o2 % 2)])
                    dma("sync", resid_dst[128 * o2:128 * (o2 + 1), 512 * tb:512 * (tb + 1)], yt,
                        [(f"{tag}yt", o2 % 2)], [("residdst", (o2, tb))])
                    if do_ln:
                        pg.op("scalar", "copy", rhold[:, o2, :], yt,
                              reads=[(f"{tag}yt", o2 % 2)], writes=[("rhold", o2)])
                        pg.op("scalar", "square", sqst, rhold[:, o2, :],
                              reads=[("rhold", o2)], writes=[("sqst", None)])
                        mm(pbank[4], c_ones128, rhold[:, o2, :],
                           [("c_ones128", None), ("rhold", o2)], [("pb4", None)],
                           start=(o2 == 0), stop=(o2 == 15))
                        mm(pbank[5], c_ones128, sqst,
                           [("c_ones128", None), ("sqst", None)], [("pb5", None)],
                           start=(o2 == 0), stop=(o2 == 15))
            if do_ln:
                ln_stats_apply(tb, [("rhold", o2, rhold[:, o2, :]) for o2 in range(16)],
                               1e-5, pbank[4], pbank[5], "pb4", "pb5")

    if do_gla:
        last = parts[-1] == "gla"
        dst = D["y"] if last else D["residA"]
        wo_phase([D["o_sp"][h] for h in range(HEADS)], "wo_a_bf",
                 lambda o2, tb: D["x"][128 * o2:128 * (o2 + 1), 512 * tb:512 * (tb + 1)],
                 dst, do_ln=not last, tag="woa")
        pg.barrier()

    def ffn_phase(call, resid_src, resid_dst, do_ln):
        tag = f"f{call}"
        wg_b = [av(0, [128, 16, 128], BF16), av(4096, [128, 16, 128], BF16)]
        wu_b = [av(8192, [128, 16, 128], BF16), av(12288, [128, 16, 128], BF16)]
        silu_sb = [av(16384, [128, T], BF16), av(20480, [128, T], BF16)]
        hstage = [av(24576, [128, T], BF16), av(28672, [128, T], BF16)]
        if call == 1:
            NU = 1408
            pkb = [av(32768, [128, 2048], I16), av(36864, [128, 2048], I16)]
            scb = [av(40960, [128, 2048], FP16), av(45056, [128, 2048], FP16)]
            dq_t = [av(49152 + 4096 * i, [128, 2048], I16) for i in range(4)]
            dqo = [av(65536, [128, 2048], BF16), av(69632, [128, 2048], BF16)]

            def dequant_unit(src16, scsrc, scname, dst_writes, uslot, nu):
                b = uslot % 2
                dma("sync", pkb[b][:, :nu], src16, [("pksrc", None)], [(f"{tag}pk", b)])
                for gi in range(4):
                    dma("scalar", scb[b][32 * gi:32 * (gi + 1), :nu],
                        scsrc[gi:gi + 1, :].partition_broadcast(32),
                        [(scname, None)], [(f"{tag}sc", b)])
                for j in range(4):
                    t0, t1, t2, t3 = (t[:, :nu] for t in dq_t)
                    dq = dqo[j % 2][:, :nu]
                    vop("tensor_scalar", t0, pkb[b][:, :nu], 4 * j, 15,
                        ALU.logical_shift_right, ALU.bitwise_and,
                        rd=[(f"{tag}pk", b)], wr=[("dqt0", None)])
                    vop("tensor_scalar", t1, t0, 7, None, ALU.bitwise_and,
                        rd=[("dqt0", None)], wr=[("dqt1", None)])
                    vop("tensor_scalar", t2, t1, 1, 1, ALU.subtract, ALU.min,
                        rd=[("dqt1", None)], wr=[("dqt2", None)])
                    vop("tensor_scalar", t1, t1, 0x200, 0x3800, ALU.mult, ALU.add,
                        rd=[("dqt1", None)], wr=[("dqt1", None)])
                    vop("tensor_scalar", t2, t2, 0x3800, 0x3800, ALU.mult, ALU.add,
                        rd=[("dqt2", None)], wr=[("dqt2", None)])
                    vop("tensor_tensor", t1, t1, t2, ALU.min,
                        rd=[("dqt1", None), ("dqt2", None)], wr=[("dqt1", None)])
                    vop("tensor_scalar", t3, t0, 8, 12, ALU.bitwise_and, ALU.logical_shift_left,
                        rd=[("dqt0", None)], wr=[("dqt3", None)])
                    vop("tensor_tensor", t1, t1, t3, ALU.bitwise_or,
                        rd=[("dqt1", None), ("dqt3", None)], wr=[("dqt1", None)])
                    vop("tensor_tensor", dq, t1.bitcast(FP16), scb[b][:, :nu], ALU.mult,
                        rd=[("dqt1", None), (f"{tag}sc", b)], wr=[("dqo", j % 2)])
                    for ci, (dst_ap, rkey) in enumerate(dst_writes[j]):
                        dma("sync", dst_ap, dq[:, 128 * ci:128 * (ci + 1)],
                            [("dqo", j % 2)], [rkey])

            for half, base_io in ((0, 0), (1, IO)):
                for u in range(4):
                    cols = slice(base_io + NU * u, base_io + NU * (u + 1))
                    for bb in range(4):
                        dw = []
                        for j in range(4):
                            ik = 4 * bb + j
                            lst = []
                            for ci in range(11):
                                iop = (half * IO + NU * u + 128 * ci) // 128
                                lst.append((D["wgu_sc"][iop][:, 128 * ik:128 * (ik + 1)],
                                            ("wgu_sc", (iop, ik))))
                            dw.append(lst)
                        dequant_unit(D["gup16"][128 * bb:128 * (bb + 1), cols],
                                     D["gsc"][4 * bb:4 * bb + 4, cols], "gsc", dw,
                                     uslot=half * 16 + u * 4 + bb, nu=NU)
            for bb in range(11):
                dw = []
                for j in range(4):
                    kp = 4 * bb + j
                    dw.append([(D["wd_sc"][o2][:, 128 * kp:128 * (kp + 1)], ("wd_sc", (o2, kp)))
                               for o2 in range(16)])
                dequant_unit(D["dp16"][128 * bb:128 * (bb + 1), :],
                             D["dsc"][4 * bb:4 * bb + 4, :], "dsc", dw,
                             uslot=bb, nu=2048)

        for p in range(NIO):
            bg, bu = wg_b[p % 2], wu_b[p % 2]
            dma("sync", bg, D["wgu_sc"][p], [("wgu_sc", (p, ik)) for ik in range(16)] if call == 1 else
                [("wgu_sc", None)], [(f"{tag}wg", p % 2)])
            dma("sync", bu, D["wgu_sc"][p + NIO],
                [("wgu_sc", (p + NIO, ik)) for ik in range(16)] if call == 1 else [("wgu_sc", None)],
                [(f"{tag}wu", p % 2)])
            sl = silu_sb[p % 2]
            for tb in range(4):
                ps = pbank[tb]
                for ik in range(16):
                    mm(ps, bg[:, ik, :], xn[:, ik, 512 * tb:512 * (tb + 1)],
                       [(f"{tag}wg", p % 2), ("xn", (ik, tb))], [(f"pb{tb}", None)],
                       start=(ik == 0), stop=(ik == 15))
                act(sl[:, 512 * tb:512 * (tb + 1)], ps, AF.Silu,
                    [(f"pb{tb}", None)], [(f"{tag}sl", p % 2)])
            hs = hstage[p % 2]
            for tb in range(4):
                ps = pbank[4 + tb]
                for ik in range(16):
                    mm(ps, bu[:, ik, :], xn[:, ik, 512 * tb:512 * (tb + 1)],
                       [(f"{tag}wu", p % 2), ("xn", (ik, tb))], [(f"pb{4+tb}", None)],
                       start=(ik == 0), stop=(ik == 15))
                vop("tensor_tensor", hs[:, 512 * tb:512 * (tb + 1)], ps,
                    sl[:, 512 * tb:512 * (tb + 1)], ALU.mult,
                    rd=[(f"pb{4+tb}", None), (f"{tag}sl", p % 2)], wr=[(f"{tag}hs", p % 2)])
            dma("sync", D["h_sp"][p], hs, [(f"{tag}hs", p % 2)], [("h_sp", p)])

        pg.barrier()

        hblk = av(0, [128, NIO, 512], BF16)
        wd_b = [av(45056, [128, NIO * 128], BF16), av(56320, [128, NIO * 128], BF16)]
        ybuf = rhold
        ytmp = av(67584, [128, 512], F32)
        sqst2 = av(69632, [128, 512], BF16)
        xres2 = [av(70656 + 2048 * i, [128, 512], F32) for i in range(3)]
        for tb in range(4):
            for kp in range(NIO):
                dma("sync", hblk[:, kp, :], D["h_sp"][kp][:, 512 * tb:512 * (tb + 1)],
                    [("h_sp", kp)], [(f"{tag}hb", kp)])
            for o2 in range(16):
                wdb = wd_b[o2 % 2]
                dma("sync", wdb, D["wd_sc"][o2],
                    [("wd_sc", (o2, kp)) for kp in range(NIO)] if call == 1 else [("wd_sc", None)],
                    [(f"{tag}wd", o2 % 2)])
                ps = pbank[o2 % 2]
                for kp in range(NIO):
                    mm(ps, wdb[:, 128 * kp:128 * (kp + 1)], hblk[:, kp, :],
                       [(f"{tag}wd", o2 % 2), (f"{tag}hb", kp)], [(f"pb{o2%2}", None)],
                       start=(kp == 0), stop=(kp == NIO - 1))
                pg.op("scalar", "copy", ybuf[:, o2, :], ps,
                      reads=[(f"pb{o2%2}", None)], writes=[("rhold", o2)])
                pg.op("scalar", "square", sqst2, ybuf[:, o2, :],
                      reads=[("rhold", o2)], writes=[("sqst2", None)])
                mm(pbank[4], c_ones128, sqst2, [("c_ones128", None), ("sqst2", None)],
                   [("pb4", None)], start=(o2 == 0), stop=(o2 == 15))
            vop("tensor_scalar", var_b, pbank[4], 1.0 / DM, 1e-6, ALU.mult, ALU.add,
                rd=[("pb4", None)], wr=[("var_b", None)])
            quake_rsqrt(rstd_b, var_b, tmp=xtmp, rd=[("var_b", None)], wr=[("rstd_b", None)])
            for o2 in range(16):
                dma("sync", xres2[o2 % 3], resid_src(o2, tb), [("residsrc", None)],
                    [(f"{tag}xr", o2 % 3)])
                vop("tensor_tensor", ytmp, ybuf[:, o2, :], rstd_b, ALU.mult,
                    rd=[("rhold", o2), ("rstd_b", None)], wr=[(f"{tag}yt2", None)])
                vop("tensor_tensor", ytmp, ytmp, xres2[o2 % 3], ALU.add,
                    rd=[(f"{tag}yt2", None), (f"{tag}xr", o2 % 3)], wr=[(f"{tag}yt2", None)])
                dma("sync", resid_dst[128 * o2:128 * (o2 + 1), 512 * tb:512 * (tb + 1)], ytmp,
                    [(f"{tag}yt2", None)], [("residdst", (o2, tb))])
                if do_ln:
                    pg.op("scalar", "copy", ybuf[:, o2, :], ytmp,
                          reads=[(f"{tag}yt2", None)], writes=[("rhold", o2)])
                    pg.op("scalar", "square", sqst2, ybuf[:, o2, :],
                          reads=[("rhold", o2)], writes=[("sqst2", None)])
                    mm(pbank[5], c_ones128, ybuf[:, o2, :],
                       [("c_ones128", None), ("rhold", o2)], [("pb5", None)],
                       start=(o2 == 0), stop=(o2 == 15))
                    mm(pbank[6], c_ones128, sqst2, [("c_ones128", None), ("sqst2", None)],
                       [("pb6", None)], start=(o2 == 0), stop=(o2 == 15))
            if do_ln:
                ln_stats_apply(tb, [("rhold", o2, ybuf[:, o2, :]) for o2 in range(16)],
                               1e-5, pbank[5], pbank[6], "pb5", "pb6")
        pg.barrier()

    if "ffn1" in parts:
        last = parts[-1] == "ffn1"
        ffn_phase(1, lambda o2, tb: D["residA"][128 * o2:128 * (o2 + 1), 512 * tb:512 * (tb + 1)],
                  D["y"] if last else D["residB"], do_ln=not last)

    if "attnb" in parts:
        v_pad = av(0, [128, 16, 4, 257], BF16)
        Pbuf = av(33792, [128, 16, 512], BF16)
        qT = [av(50176, [128, T], BF16), av(54272, [128, T], BF16)]
        kT = [av(57344 + 1024, [128, T], BF16), av(62464, [128, T], BF16)]
        wqk_b = [av(66560, [128, 16, 128], BF16), av(70656, [128, 16, 128], BF16)]
        u1b = av(74752, [128, 4, 257], F32)
        a_f = av(78976, [128, 4, 256], F32)
        abf = av(83072, [128, 4, 256], BF16)
        aTst = av(85120, [128, 2, 512], BF16)
        r_sm = av(87168, [128, 24], F32)
        wvres = av(50176, [128, 16, 1024], BF16)

        def quake_recip(dst, srcname, src, tmp, rd):
            vop("tensor_scalar", dst.bitcast(I32), src.bitcast(I32), -1, 0x7EF311C3,
                ALU.mult, ALU.add, rd=rd, wr=[("r_sm", None)])
            for _ in range(3):
                vop("tensor_tensor", tmp, dst, src, ALU.mult,
                    rd=[("r_sm", None)] + rd, wr=[("r_sm", None)])
                vop("tensor_scalar", tmp, tmp, -1.0, 2.0, ALU.mult, ALU.add,
                    rd=[("r_sm", None)], wr=[("r_sm", None)])
                vop("tensor_tensor", dst, dst, tmp, ALU.mult,
                    rd=[("r_sm", None)], wr=[("r_sm", None)])

        for half in range(2):
            pg.barrier()
            for ik in range(16):
                dma("gpsimd", wvres[:, ik, :],
                    D["wv_b"][128 * ik:128 * (ik + 1), 1024 * half:1024 * (half + 1)],
                    [("wv_b", None)], [("wvres", ik)])
            for it in range(16):
                for ob in range(2):
                    ps = pbank[2 * (it % 2) + ob]
                    pk = f"pb{2*(it%2)+ob}"
                    for ik in range(16):
                        mm(ps, xn[:, ik, 128 * it:128 * (it + 1)], wvres[:, ik, 512 * ob:512 * (ob + 1)],
                           [("wvres", ik), ("xn", (ik, it // 4))], [(pk, None)],
                           start=(ik == 0), stop=(ik == 15))
                    vop("tensor_copy", v_pad[:, it, 2 * ob:2 * ob + 2, 0:256],
                        ps.rearrange("p (a b) -> p a b", b=256),
                        rd=[(pk, None)], wr=[("v_pad", it)])
                vop("tensor_copy", v_pad[:, it, :, 256:257], c_ones128[:, 0:4].unsqueeze(2),
                    rd=[("c_ones128", None)], wr=[("v_pad", it)])
            pg.barrier()
            for hh in range(4):
                h = 4 * half + hh
                for br in range(2):
                    wb = wqk_b[0]
                    dma("gpsimd", wb, D["wq_b"][2 * h + br].rearrange("(a p) o -> p a o", p=128),
                        [("wq_b", None)], [("wqk", 0)])
                    for tb in range(4):
                        ps = pbank[4 + tb]
                        for ik in range(16):
                            mm(ps, wb[:, ik, :], xn[:, ik, 512 * tb:512 * (tb + 1)],
                               [("wqk", 0), ("xn", (ik, tb))], [(f"pb{4+tb}", None)],
                               start=(ik == 0), stop=(ik == 15))
                        pg.op("scalar", "copy", qT[br][:, 512 * tb:512 * (tb + 1)], ps,
                              reads=[(f"pb{4+tb}", None)], writes=[(f"qT{br}", None)])
                    wb = wqk_b[1]
                    dma("gpsimd", wb, D["wk_b"][2 * h + br].rearrange("(a p) o -> p a o", p=128),
                        [("wk_b", None)], [("wqk", 1)])
                    for tb in range(4):
                        ps = pbank[4 + tb]
                        for ik in range(16):
                            mm(ps, wb[:, ik, :], xn[:, ik, 512 * tb:512 * (tb + 1)],
                               [("wqk", 1), ("xn", (ik, tb))], [(f"pb{4+tb}", None)],
                               start=(ik == 0), stop=(ik == 15))
                        pg.op("scalar", "copy", kT[br][:, 512 * tb:512 * (tb + 1)], ps,
                              reads=[(f"pb{4+tb}", None)], writes=[(f"kT{br}", None)])
                for qb in range(4):
                    for br in range(2):
                        for ktt in range(4 * qb + 4):
                            ps = pbank[4 + ktt % 2]
                            pk = f"pb{4+ktt%2}"
                            mm(ps, kT[br][:, 128 * ktt:128 * (ktt + 1)],
                               qT[br][:, 512 * qb:512 * (qb + 1)],
                               [(f"kT{br}", None), (f"qT{br}", None)], [(pk, None)])
                            act(Pbuf[:, ktt, :], ps, AF.Exp, [(pk, None)], [("Pbuf", ktt)],
                                scale=B_SCALE)
                            off = 128 * ktt - 512 * qb
                            if off >= 0:
                                s0 = 512 - off
                                vop("tensor_tensor", Pbuf[:, ktt, :], Pbuf[:, ktt, :],
                                    c_stair[:, s0:s0 + 512], ALU.mult,
                                    rd=[("Pbuf", ktt), ("c_stair", None)], wr=[("Pbuf", ktt)])
                        for qi in range(4):
                            qt = 4 * qb + qi
                            ps = pbank[qi]
                            for ktt in range(qt + 1):
                                mm(ps[:, 0:257], Pbuf[:, ktt, 128 * qi:128 * (qi + 1)],
                                   v_pad[:, ktt, hh, :],
                                   [("Pbuf", ktt), ("v_pad", ktt)], [(f"pb{qi}", None)],
                                   start=(ktt == 0), stop=(ktt == qt))
                            if br == 0:
                                vop("tensor_copy", u1b[:, qi, :], ps[:, 0:257],
                                    rd=[(f"pb{qi}", None)], wr=[("u1b", qi)])
                    quake_recip(r_sm[:, 0:4], "u1b", u1b[:, :, 256], r_sm[:, 16:20],
                                [("u1b", 0), ("u1b", 1), ("u1b", 2), ("u1b", 3)])
                    for qi in range(4):
                        vop("tensor_copy", r_sm[:, 20 + qi:21 + qi], pbank[qi][:, 256:257],
                            rd=[(f"pb{qi}", None)], wr=[("r_sm", None)])
                    quake_recip(r_sm[:, 4:8], "r_sm", r_sm[:, 20:24], r_sm[:, 16:20], [("r_sm", None)])
                    vop("tensor_scalar", r_sm[:, 4:8], r_sm[:, 4:8], lam_cols[:, 0:1], None,
                        ALU.mult, rd=[("r_sm", None), ("lam_cols", None)], wr=[("r_sm", None)])
                    for qi in range(4):
                        vop("tensor_scalar", a_f[:, qi, :], u1b[:, qi, 0:256], r_sm[:, qi:qi + 1],
                            None, ALU.mult, rd=[("u1b", qi), ("r_sm", None)], wr=[("a_f", qi)])
                        vop("scalar_tensor_tensor", a_f[:, qi, :], pbank[qi][:, 0:256],
                            r_sm[:, 4 + qi:5 + qi], a_f[:, qi, :], ALU.mult, ALU.add,
                            rd=[(f"pb{qi}", None), ("r_sm", None), ("a_f", qi)], wr=[("a_f", qi)])
                        vop("tensor_tensor", u1b[:, qi, 0:256], a_f[:, qi, :], a_f[:, qi, :],
                            ALU.mult, rd=[("a_f", qi)], wr=[("u1b", qi)])
                        vop("reduce_sum", r_sm[:, 8 + qi:9 + qi].unsqueeze(2),
                            u1b[:, qi:qi + 1, 0:256], AX.X, rd=[("u1b", qi)], wr=[("r_sm", None)])
                    vop("tensor_scalar", r_sm[:, 8:12], r_sm[:, 8:12], 1.0 / 256, 1e-5,
                        ALU.mult, ALU.add, rd=[("r_sm", None)], wr=[("r_sm", None)])
                    quake_rsqrt(r_sm[:, 12:16], r_sm[:, 8:12], tmp=r_sm[:, 16:20],
                                rd=[("r_sm", None)], wr=[("r_sm", None)], tmp_key=("r_sm", None))
                    for qi in range(4):
                        vop("tensor_scalar", abf[:, qi, :], a_f[:, qi, :], r_sm[:, 12 + qi:13 + qi],
                            None, ALU.mult, rd=[("a_f", qi), ("r_sm", None)], wr=[("abf", qi)])
                    for qi in range(4):
                        for vt in range(2):
                            bk = 6 + qi % 2
                            pv = pbank[bk][:, 0:64].bitcast(BF16)
                            trans(pv, abf[:, qi, 128 * vt:128 * (vt + 1)],
                                  [("abf", qi)], [(f"pb{bk}", None)])
                            vop("tensor_copy", aTst[:, vt, 128 * qi:128 * (qi + 1)], pv,
                                rd=[(f"pb{bk}", None)], wr=[("aTst", vt)])
                    for vt in range(2):
                        dma("sync", D["a_sp"][h, vt][:, 512 * qb:512 * (qb + 1)], aTst[:, vt, :],
                            [("aTst", vt)], [("a_sp", (h, vt, qb))])
        pg.barrier()
        last = parts[-1] == "attnb"
        wo_phase([D["a_sp"][hh, vt] for hh in range(BH) for vt in range(2)], "wo_b_bf",
                 lambda o2, tb: D["residB"][128 * o2:128 * (o2 + 1), 512 * tb:512 * (tb + 1)],
                 D["y"] if last else D["residA"], do_ln=not last, tag="wob")
        pg.barrier()

    if "ffn2" in parts:
        ffn_phase(2, lambda o2, tb: D["residA"][128 * o2:128 * (o2 + 1), 512 * tb:512 * (tb + 1)],
                  D["y"], do_ln=False)

    pg.barrier()

    es = contextlib.ExitStack()
    sems = {}
    for d in ENGS + QUEUES:
        sems[d] = es.enter_context(nc.semaphore("sem_" + d))
    pg.emit(nc, sems)
    return nc


_CACHE = {}


def kernel(**inputs):
    parts = ("gla", "ffn1", "attnb", "ffn2")
    if "nc" not in _CACHE:
        _CACHE["nc"] = build_nc(parts)
    nc = _CACHE["nc"]
    shared = _host_pack(inputs)
    cast = {}
    for k, v in shared.items():
        a = np.asarray(v)
        if a.dtype == np.float64:
            a = a.astype(np.float32)
        cast[k] = a
    in_maps = []
    for core in range(8):
        m = dict(cast)
        m.update(_host_per_core(inputs, core))
        in_maps.append(m)
    res = run_bass_kernel_spmd(nc, in_maps, core_ids=list(range(8)))
    out = np.empty((8, T, DM), np.float32)
    for core in range(8):
        y = res.results[core]["y"]
        out[core][:, SIG] = y.T
    return out
